# version 01
# baseline (speedup 1.0000x reference)
import numpy as np
from contextlib import ExitStack

import concourse.bass as bass
import concourse.mybir as mybir
from concourse.bass_utils import run_bass_kernel_spmd

F32 = mybir.dt.float32
AF = mybir.ActivationFunctionType
OP = mybir.AluOpType
AX = mybir.AxisListType

D = 1024
DI = 2048
NH = 32
DS = 128
DIP = 2 * DI + 2 * DS + NH
SEQ = 2
L = 2048
TT = 512
NCHS = L // TT
NCH = SEQ * NCHS
SC = 128
NSC = TT // SC
NKD = D // 128
MW = [128] * 34 + [32]
EPS = 1e-5
NEG = -1e30
XW = TT + 3

ENGINES = ("tensor", "vector", "scalar", "sync", "gpsimd")


class Sem:
    __slots__ = ("h", "count", "name")
    def __init__(self, h, name):
        self.h, self.count, self.name = h, 0, name
    @property
    def tok(self):
        return (self, self.count)


def tmax(*toks):
    d = {}
    for t in toks:
        if not t:
            continue
        if isinstance(t, list):
            for e_ in t:
                if e_ is None:
                    continue
                s, v = e_
                if d.get(s, 0) < v:
                    d[s] = v
        else:
            s, v = t
            if d.get(s, 0) < v:
                d[s] = v
    return [(s, v) for s, v in d.items()]


class Plan:
    def __init__(self):
        self.ops = {e: [] for e in ENGINES}
        self.last_wait = {e: {} for e in ENGINES}

    def emit(self, eng, fn, waits=(), inc=None, incn=1):
        lw = self.last_wait[eng]
        real = []
        for s, v in tmax(list(waits)):
            if v <= 0 or lw.get(s.name, 0) >= v:
                continue
            lw[s.name] = v
            real.append((s, v))
        self.ops[eng].append((fn, real, inc, incn))
        if inc is not None:
            inc.count += incn
            return inc.tok
        return None


class Buf:
    __slots__ = ("w", "r")
    def __init__(self):
        self.w, self.r = [], []
    def wr(self):
        return tmax(self.w, self.r)
    def rd(self):
        return tmax(self.w)
    def wrote(self, tok):
        self.w = tmax(self.w, [tok] if isinstance(tok, tuple) else tok)
        self.r = []
    def read(self, tok):
        self.r = tmax(self.r, [tok] if isinstance(tok, tuple) else tok)


def _pack_colslab(w, widths):
    K = w.shape[0]
    nk = K // 128
    parts, off = [], 0
    for wm in widths:
        sl = w[:, off:off + wm].reshape(nk, 128, wm).transpose(1, 0, 2).reshape(128, nk * wm)
        parts.append(sl)
        off += wm
    return np.ascontiguousarray(np.concatenate(parts, axis=1))


def _cols(v):
    n = v.shape[0] // 128
    return np.ascontiguousarray(v.reshape(n, 128).T)


def host_pack(inp):
    g = {}
    g["w_inproj"] = _pack_colslab(np.asarray(inp["L0_attn_in_proj_w"], np.float32), MW)
    g["w_outproj"] = _pack_colslab(np.asarray(inp["L0_attn_out_proj_w"], np.float32), [128] * 8)
    g["w_mlp1"] = _pack_colslab(np.asarray(inp["L0_attn_mlp_w1"], np.float32), [128] * 32)
    g["w_mlp2"] = _pack_colslab(np.asarray(inp["L0_attn_mlp_w2"], np.float32), [128] * 8)
    g["w_ffnT"] = np.ascontiguousarray(np.asarray(inp["L0_ffn_W"], np.float32).T)
    g["lora_downT"] = np.ascontiguousarray(np.asarray(inp["L0_ffn_lora_down"], np.float32).T)
    g["down_aux"] = np.ascontiguousarray(np.asarray(inp["L0_ffn_down_aux"], np.float32))
    g["lora_upT"] = np.ascontiguousarray(np.asarray(inp["L0_ffn_lora_up"], np.float32).T)
    g["up_auxT"] = np.ascontiguousarray(np.asarray(inp["L0_ffn_up_aux"], np.float32).T)
    g["conv_w"] = np.ascontiguousarray(
        np.asarray(inp["L0_attn_conv_w"], np.float32).reshape(18, 128, 4).transpose(1, 0, 2).reshape(128, 72))
    g["conv_b"] = _cols(np.asarray(inp["L0_attn_conv_b"], np.float32))
    g["dt_bias"] = np.ascontiguousarray(np.asarray(inp["L0_attn_dt_bias"], np.float32).reshape(32, 1))
    g["A_log"] = np.ascontiguousarray(np.asarray(inp["L0_attn_A_log"], np.float32).reshape(32, 1))
    dsk = np.asarray(inp["L0_attn_D_skip"], np.float32)
    g["dskip2"] = np.ascontiguousarray(np.stack(
        [np.concatenate([np.full(64, dsk[2 * f]), np.full(64, dsk[2 * f + 1])]) for f in range(16)], axis=1))
    g["rms_w"] = _cols(np.asarray(inp["L0_attn_rms_w"], np.float32))
    g["ln_w"] = _cols(np.asarray(inp["L0_attn_ln_w"], np.float32))
    g["ln_b"] = _cols(np.asarray(inp["L0_attn_ln_b"], np.float32))
    g["b1"] = _cols(np.asarray(inp["L0_attn_mlp_b1"], np.float32))
    g["ffn_bb2"] = np.concatenate([_cols(np.asarray(inp["L0_attn_mlp_b2"], np.float32)),
                                   _cols(np.asarray(inp["L0_ffn_b"], np.float32))], axis=1)
    g["ident"] = np.eye(128, dtype=np.float32)
    s_ = np.arange(128).reshape(128, 1)
    t_ = np.arange(128).reshape(1, 128)
    g["mask4"] = np.ascontiguousarray(np.tile(np.where(t_ >= s_, 0.0, NEG).astype(np.float32), (1, 4)))
    g["onesrow"] = np.ones((1, NH * SC), dtype=np.float32)
    g["onesblk"] = np.ones((128, 128), dtype=np.float32)
    g["epscol"] = np.concatenate([np.full((128, 1), EPS, np.float32), np.ones((128, 1), np.float32)], axis=1)
    return g


INPUT_SPECS = [
    ("x", [SEQ, L, D]), ("w_inproj", [128, 8 * DIP]), ("w_outproj", [128, 16 * D]),
    ("w_mlp1", [128, 8 * 4096]), ("w_mlp2", [128, 32 * D]), ("w_ffnT", [D, D]),
    ("lora_downT", [100, 4]), ("down_aux", [100, D]), ("lora_upT", [4, 50]),
    ("up_auxT", [50, D]), ("conv_w", [128, 72]), ("conv_b", [128, 18]),
    ("dt_bias", [32, 1]), ("A_log", [32, 1]), ("dskip2", [128, 16]),
    ("rms_w", [128, 16]), ("ln_w", [128, 8]), ("ln_b", [128, 8]), ("b1", [128, 32]),
    ("ffn_bb2", [128, 16]), ("ident", [128, 128]), ("mask4", [128, 4 * SC]),
    ("onesrow", [1, NH * SC]), ("onesblk", [128, 128]), ("epscol", [128, 2]),
]


def build_program(taps=None, nch=NCH, stop_phase=99):
    nc = bass.Bass("TRN2", target_bir_lowering=False)
    dram = {}
    for name, shape in INPUT_SPECS:
        dram[name] = nc.dram_tensor(name, shape, F32, kind="ExternalInput").ap()
    out_d = nc.dram_tensor("out", [SEQ, L, D], F32, kind="ExternalOutput").ap()
    weff_s = nc.dram_tensor("weff_s", [128, 8 * D], F32).ap()
    tap_outs = {}
    if taps:
        for tname, shape, _ in taps:
            tap_outs[tname] = nc.dram_tensor(tname, shape, F32, kind="ExternalOutput").ap()

    es = ExitStack()
    def sbuf(name, shape):
        return es.enter_context(nc.sbuf_tensor(name, shape, F32))

    Csb = sbuf("Csb", [128, SC])
    tmA = sbuf("tmA", [128, 4 * D])
    tmB = sbuf("tmB", [128, 4 * D])
    fmS = sbuf("fmS", [128, 4096])
    Ybuf = sbuf("Ybuf", [128, 16 * TT])
    xbc = sbuf("xbc", [128, 18 * XW])
    pool1 = sbuf("pool1", [32, 3584])
    PA, PB, PC, PD, PE_, PF, PG = 0, 512, 1024, 1536, 2048, 2560, 3072
    dtT = sbuf("dtT", [128, 64])
    flat = sbuf("flat", [128, 3072])
    XdT = sbuf("XdT", [128, 16 * SC])
    MgA = sbuf("MgA", [128, 1024])
    MgB = sbuf("MgB", [128, 1024])
    CEg = sbuf("CEg", [128, 1024])
    Gsb = sbuf("Gsb", [128, SC])
    BTsb = sbuf("BTsb", [128, DS])
    wXd = sbuf("wXd", [128, 2 * SC])
    ST = sbuf("ST", [128, 2 * 16 * SC])
    ebc = sbuf("ebc", [128, NH])
    waren = sbuf("waren", [128, 3 * 1024])
    szsb = sbuf("szsb", [128, 1024])
    sqsb = sbuf("sqsb", [128, 1024])
    colp = sbuf("colp", [128, 40])
    maskb = sbuf("maskb", [128, 4 * SC])
    onesb = sbuf("onesb", [128, 128])
    cns = sbuf("cns", [128, 320])
    CID, CDSK, CCB, CCW, CRW, CB1, CBB, CGAM, CBET, CA = 0, 128, 144, 162, 234, 250, 282, 298, 306, 314
    lda = XdT[0:100, 0:1024]
    lupT = sqsb[0:4, 0:50]
    ldnT = sqsb[0:100, 64:68]
    uaT = szsb[0:50, 0:1024]
    ps = [es.enter_context(nc.psum_tensor(f"ps{i}", [128, 512], F32)) for i in range(8)]

    sems = {n: Sem(es.enter_context(nc.semaphore(n)), n) for n in ("sPE", "sDVE", "sACT", "sWL", "sIO")}
    sPE, sDVE, sACT, sWL, sIO = (sems[k] for k in ("sPE", "sDVE", "sACT", "sWL", "sIO"))

    P = Plan()
    bufs = {n: Buf() for n in ("tmA", "tmB", "fmS", "Ybuf", "xbc", "pool1", "dtT", "flat",
                               "XdT", "Gsb", "BTsb", "ebc", "colp", "szsb", "sqsb", "wXd")}

    def dma(queue, dst, src, waits=()):
        return P.emit(queue, lambda e, d=dst, s=src: e.dma_start(out=d, in_=s),
                      waits=waits, inc=(sWL if queue == "sync" else sIO), incn=16)

    def mm(o, l, r, start, stop, waits=(), inc=None):
        return P.emit("tensor", lambda e, o=o, l=l, r=r, a=start, b=stop:
                      e.matmul(o, l, r, start=a, stop=b), waits=waits, inc=inc)

    def tr(o, i, kdim, waits=(), inc=None, base=0):
        ident = cns[base:base + kdim, CID + base:CID + base + kdim]
        return P.emit("tensor", lambda e, o=o, i=i, d=ident: e.transpose(o, i, d),
                      waits=waits, inc=inc)

    def dve(fn, waits=(), inc=sDVE):
        return P.emit("vector", fn, waits=waits, inc=inc)

    def act(fn, waits=(), inc=sACT):
        return P.emit("scalar", fn, waits=waits, inc=inc)

    bank_tok = [[] for _ in range(8)]
    held = set()
    bank_i = [0]
    def bank_next():
        for _ in range(9):
            i = bank_i[0] % 8
            bank_i[0] += 1
            if i not in held:
                return i
        raise RuntimeError("no banks")

    slab_tok = [[], [], []]
    slab_i = [0]
    def slab_next():
        i = slab_i[0] % 3
        slab_i[0] += 1
        return i, waren[:, i * 1024:(i + 1) * 1024]

    def bcN(ap128, n):
        return bass.AP(ap128.tensor, ap128.offset, [ap128.ap[0], [0, n], [1, 128]])

    def r4(ap):
        return ap.rearrange("p (r f) -> p r f", r=4)

    c0 = []
    for dst, src in ((cns[:, CID:CID + 128], "ident"), (cns[:, CDSK:CDSK + 16], "dskip2"),
                     (cns[:, CCB:CCB + 18], "conv_b"), (cns[:, CCW:CCW + 72], "conv_w"),
                     (cns[:, CRW:CRW + 16], "rms_w"), (cns[:, CB1:CB1 + 32], "b1"),
                     (cns[:, CBB:CBB + 16], "ffn_bb2"), (cns[:, CGAM:CGAM + 8], "ln_w"),
                     (cns[:, CBET:CBET + 8], "ln_b")):
        c0.append(dma("gpsimd", dst, dram[src]))
    c0.append(dma("gpsimd", cns[:, 318:320], dram["epscol"]))
    c0.append(dma("gpsimd", cns[0:32, CA + 2:CA + 3], dram["dt_bias"]))
    c0.append(dma("gpsimd", cns[0:32, CA + 3:CA + 4], dram["A_log"]))
    c0.append(dma("gpsimd", maskb[:], dram["mask4"]))
    c0.append(dma("gpsimd", onesb[:], dram["onesblk"]))
    c0.append(dma("gpsimd", flat[1:2, 0:1024], dram["onesrow"][0:1, 0:1024]))
    c0.append(dma("gpsimd", flat[0:1, 1024:2048], dram["onesrow"][0:1, 0:1024]))
    c0.append(dma("gpsimd", lda, dram["down_aux"]))
    c0.append(dma("gpsimd", ldnT, dram["lora_downT"]))
    c0.append(dma("gpsimd", lupT, dram["lora_upT"]))
    c0.append(dma("gpsimd", uaT, dram["up_auxT"]))
    t_c0 = tmax(*c0)
    t_na = act(lambda e: e.activation(cns[0:32, CA + 1:CA + 2], cns[0:32, CA + 3:CA + 4], AF.Exp),
               waits=t_c0)
    t_A = dve(lambda e: e.tensor_scalar_mul(cns[0:32, CA:CA + 1], cns[0:32, CA + 1:CA + 2], -1.0),
              waits=[t_na])
    t_const = tmax(t_c0, [t_na], [t_A])

    b0, b1_ = bank_next(), bank_next()
    mA = mm(ps[b0][0:4, 0:512], ldnT, lda[:, 0:512], True, True, waits=t_c0)
    mB = mm(ps[b1_][0:4, 0:512], ldnT, lda[:, 512:1024], True, True, waits=(), inc=sPE)
    eA = dve(lambda e, b0=b0: e.tensor_copy(xbc[0:4, 0:512], ps[b0][0:4, 0:512]), waits=[mB])
    eB = dve(lambda e, b1_=b1_: e.tensor_copy(xbc[0:4, 512:1024], ps[b1_][0:4, 0:512]))
    bank_tok[b0], bank_tok[b1_] = [eA], [eB]
    b0, b1_ = bank_next(), bank_next()
    mA = mm(ps[b0][0:50, 0:512], lupT, xbc[0:4, 0:512], True, True,
            waits=tmax([eA], [eB], bank_tok[b0], bank_tok[b1_]))
    mB = mm(ps[b1_][0:50, 0:512], lupT, xbc[0:4, 512:1024], True, True, inc=sPE)
    eC = dve(lambda e, b0=b0: e.tensor_copy(xbc[0:50, 1024:1536], ps[b0][0:50, 0:512]), waits=[mB])
    eD = dve(lambda e, b1_=b1_: e.tensor_copy(xbc[0:50, 1536:2048], ps[b1_][0:50, 0:512]))
    bank_tok[b0], bank_tok[b1_] = [eC], [eD]
    lil_tok = []
    for mo in range(8):
        bL, bR = bank_next(), bank_next()
        m1_ = mm(ps[bL][:, 0:512], uaT[:, mo * 128:(mo + 1) * 128], xbc[0:50, 1024:1536], True, True,
                 waits=tmax([eC], [eD], bank_tok[bL], bank_tok[bR]))
        m2_ = mm(ps[bR][:, 0:512], uaT[:, mo * 128:(mo + 1) * 128], xbc[0:50, 1536:2048], True, True,
                 inc=sPE)
        e1_ = dve(lambda e, mo=mo, bL=bL: e.tensor_copy(
            Ybuf[:, mo * 1024:mo * 1024 + 512], ps[bL][:, 0:512]), waits=[m2_])
        e2_ = dve(lambda e, mo=mo, bR=bR: e.tensor_copy(
            Ybuf[:, mo * 1024 + 512:(mo + 1) * 1024], ps[bR][:, 0:512]))
        bank_tok[bL], bank_tok[bR] = [e1_], [e2_]
        lil_tok.append(tmax([e1_], [e2_]))
    prev_store = []
    prev_reads = []
    for ki in range(8):
        tw = dma("sync", waren[:, 0:1024], dram["w_ffnT"][ki * 128:(ki + 1) * 128, :],
                 waits=prev_reads)
        accs = []
        for mo in range(8):
            b_ = bank_next()
            tt = tr(ps[b_][:, 0:128],
                    Ybuf[:, mo * 1024 + ki * 128:mo * 1024 + (ki + 1) * 128], 128,
                    waits=tmax(lil_tok[mo], bank_tok[b_], t_c0), inc=sPE)
            ee = dve(lambda e, b_=b_, mo=mo: e.tensor_add(
                waren[:, 1024 + mo * 128:1024 + (mo + 1) * 128], ps[b_][:, 0:128],
                waren[:, mo * 128:(mo + 1) * 128]),
                waits=tmax([tt], [tw], prev_store))
            bank_tok[b_] = [ee]
            accs.append(ee)
        wbase = weff_s[:, ki * 128:ki * 128 + 128]
        wout = bass.AP(wbase.tensor, wbase.offset, [wbase.ap[0], [1024, 8], [1, 128]])
        td = dma("gpsimd", wout, waren[:, 1024:2048].rearrange("p (r f) -> p r f", r=8),
                 waits=tmax(*accs))
        prev_store = [td]
        prev_reads = tmax(*accs)
    t_weff_done = prev_store
    slab_tok[0] = prev_reads
    slab_tok[1] = prev_store
    g_start = [(sPE, sPE.count), (sDVE, sDVE.count), (sIO, sIO.count)]
    bufs["XdT"].wrote(g_start)
    bufs["szsb"].wrote(g_start)
    bufs["sqsb"].wrote(g_start)

    gam = lambda j: cns[:, CGAM + j:CGAM + j + 1]
    bet = lambda j: cns[:, CBET + j:CBET + j + 1]

    def xbc_t(c, a, b):
        return xbc[:, c * XW + a:c * XW + b]

    def ln_stats(src, extra_w=()):
        sw = tmax(list(extra_w))
        tsum, tsq = [], []
        for g in range(4):
            tsum.append(dve(lambda e, g=g: e.reduce_sum(colp[:, g:g + 1], src[:, g * D:(g + 1) * D],
                                                        axis=AX.X), waits=tmax(sw, bufs["colp"].wr())))
            tsq.append(act(lambda e, g=g: e.activation(szsb[:, 0:1024], src[:, g * D:(g + 1) * D],
                                                       AF.Square, accum_out=colp[:, 4 + g:5 + g]),
                           waits=tmax(sw, bufs["szsb"].wr())))
        ta = dve(lambda e: e.tensor_scalar_mul(colp[:, 8:12], colp[:, 0:4], 1.0 / D), waits=tmax(tsum))
        tb = act(lambda e: e.activation(colp[:, 12:16], colp[:, 4:8], AF.Identity, scale=1.0 / D))
        tc = dve(lambda e: e.tensor_tensor(colp[:, 16:20], colp[:, 8:12], colp[:, 8:12], OP.mult),
                 waits=[ta])
        td = dve(lambda e: e.tensor_sub(colp[:, 20:24], colp[:, 12:16], colp[:, 16:20]),
                 waits=tmax([tb], [tc]))
        te = act(lambda e: e.activation(colp[:, 24:28], colp[:, 20:24], AF.Sqrt, bias=cns[:, 318:319]),
                 waits=tmax([td], t_c0))
        tf = dve(lambda e: e.reciprocal(colp[:, 28:32], colp[:, 24:28]), waits=[te])
        bufs["szsb"].wrote(tmax(*tsq))
        return tf, tmax(*tsq)

    def ln_apply(dst, src, extra_w=()):
        toks = []
        for g in range(4):
            toks.append(dve(lambda e, g=g: e.tensor_scalar(
                dst[:, g * D:(g + 1) * D], src[:, g * D:(g + 1) * D],
                colp[:, 8 + g:9 + g], colp[:, 28 + g:29 + g], OP.subtract, OP.mult),
                waits=tmax(list(extra_w))))
        return toks

    def transpose_1024(src, evac, waits_in=(), fm_src=False):
        toks = []
        for g in range(4):
            for jj in range(0, 8, 4):
                b_ = bank_next()
                tok_tr = None
                for q in range(4):
                    j = jj + q
                    sl = (src[:, j * TT + g * 128:j * TT + (g + 1) * 128] if fm_src
                          else src[:, g * D + j * 128:g * D + (j + 1) * 128])
                    tok_tr = tr(ps[b_][:, q * 128:(q + 1) * 128], sl, 128,
                                waits=tmax(list(waits_in), bank_tok[b_], t_const) if q == 0 else (),
                                inc=(sPE if q == 3 else None))
                evs = [evac(b_, q, jj + q, g, tok_tr) for q in range(4)]
                bank_tok[b_] = [evs[-1]]
                toks += evs
        return toks

    st_state = {"ping": 0, "ready": []}
    flat_users = [[]]
    t_out_final = [[]]

    for ch in range(nch):
        seq, ci = divmod(ch, NCHS)
        t0 = ci * TT
        first = (ci == 0)

        tin = []
        for g in range(4):
            tin.append(dma("gpsimd", tmA[:, g * D:(g + 1) * D],
                           dram["x"][seq, t0 + g * 128:t0 + (g + 1) * 128, :],
                           waits=bufs["tmA"].wr()))
        bufs["tmA"].wrote(tmax(*tin))
        if first:
            guard = [(sPE, sPE.count)]
            tz = []
            for c in range(18):
                tz.append(dve(lambda e, c=c: e.memset(xbc_t(c, 0, 3), 0.0),
                              waits=tmax(bufs["xbc"].wr(), guard)))
            bufs["xbc"].wrote(tmax(*tz))
            pg = st_state["ping"]
            tz2 = dve(lambda e, pg=pg: e.memset(ST[:, pg * 2048:(pg + 1) * 2048], 0.0), waits=guard)
            st_state["ready"] = [tz2]

        tf, _ = ln_stats(tmA, bufs["tmA"].rd())
        tapply = ln_apply(tmB, tmA, tmax([tf], bufs["tmA"].rd(), bufs["tmB"].wr()))
        bufs["colp"].wrote(tmax([tf], *tapply))
        bufs["tmB"].wrote(tmax(*tapply))
        bufs["tmA"].read(tmax(*tapply))

        if stop_phase <= 1:
            continue
        def ev_xn(b_, q, j, g, tok_tr):
            return dve(lambda e: e.tensor_scalar(
                fmS[:, j * TT + g * 128:j * TT + (g + 1) * 128], ps[b_][:, q * 128:(q + 1) * 128],
                gam(j), bet(j), OP.mult, OP.add),
                waits=tmax([tok_tr], bufs["fmS"].wr()))
        tr_ev = transpose_1024(tmB, ev_xn, waits_in=tmax(bufs["tmB"].rd()))
        bufs["tmB"].read(tmax(*tr_ev))
        bufs["fmS"].wrote(tmax(*tr_ev))
        t_xn = tmax(*tr_ev)

        def inproj_m(m, evac_fn):
            wm = MW[m]
            off = 8 * sum(MW[:m])
            si, slot = slab_next()
            tok_w = dma("sync", slot[:, 0:8 * wm], dram["w_inproj"][:, off:off + 8 * wm],
                        waits=slab_tok[si])
            b_ = bank_next()
            last = None
            for k in range(NKD):
                last = mm(ps[b_][0:wm, 0:TT], slot[:, k * wm:(k + 1) * wm], fmS[:, k * TT:(k + 1) * TT],
                          start=(k == 0), stop=(k == NKD - 1),
                          waits=tmax([tok_w], bank_tok[b_], t_xn) if k == 0 else (),
                          inc=(sPE if k == NKD - 1 else None))
            ev = evac_fn(b_, last)
            bank_tok[b_] = [ev]
            slab_tok[si] = [last]
            return ev

        if stop_phase <= 2:
            continue
        ev_xbc = []
        for c in range(18):
            def evac(b_, last, c=c):
                w_ = tmax([last], bufs["xbc"].wr())
                if c % 2 == 0:
                    return dve(lambda e: e.tensor_copy(xbc_t(c, 3, 3 + TT), ps[b_][:, 0:TT]), waits=w_)
                return act(lambda e: e.activation(xbc_t(c, 3, 3 + TT), ps[b_][:, 0:TT], AF.Identity),
                           waits=w_)
            ev_xbc.append(inproj_m(16 + c, evac))
        ev_dt = inproj_m(34, lambda b_, last: dve(
            lambda e: e.tensor_scalar(pool1[0:32, PA:PA + TT], ps[b_][0:32, 0:TT],
                                      cns[0:32, CA + 2:CA + 3], None, OP.add),
            waits=tmax([last], bufs["pool1"].wr(), t_const)))
        bufs["xbc"].wrote(tmax(*ev_xbc))

        if stop_phase <= 3:
            continue
        conv_toks = []
        for c in range(18):
            sl = szsb[:, (c % 2) * 512:(c % 2) * 512 + TT]
            tcv = dve(lambda e, c=c, sl=sl: e.tensor_scalar(
                sl, xbc_t(c, 0, TT), cns[:, CCW + c * 4:CCW + c * 4 + 1], cns[:, CCB + c:CCB + c + 1],
                OP.mult, OP.add),
                waits=tmax([ev_xbc[c]], bufs["szsb"].wr(), t_const,
                           [conv_toks[c - 2]] if c >= 2 else []))
            for k in range(1, 4):
                tcv = dve(lambda e, c=c, k=k, sl=sl: e.scalar_tensor_tensor(
                    sl, xbc_t(c, k, k + TT), cns[:, CCW + c * 4 + k:CCW + c * 4 + k + 1], sl,
                    OP.mult, OP.add), waits=[tcv])
            tmg = dve(lambda e, c=c: e.tensor_copy(xbc_t(c, 0, 3), xbc_t(c, TT, TT + 3)), waits=[tcv])
            tsl = act(lambda e, c=c, sl=sl: e.activation(xbc_t(c, 3, 3 + TT), sl, AF.Silu),
                      waits=tmax([tcv], [tmg]))
            bufs["szsb"].wrote(tcv)
            conv_toks.append(tsl)
        bufs["xbc"].wrote(tmax(*conv_toks))

        if stop_phase <= 4:
            continue
        t1_ = act(lambda e: e.activation(pool1[0:32, PB:PB + TT], pool1[0:32, PA:PA + TT], AF.Abs),
                  waits=tmax([ev_dt], bufs["pool1"].wr()))
        t2_ = act(lambda e: e.activation(pool1[0:32, PC:PC + TT], pool1[0:32, PB:PB + TT], AF.Exp,
                                         scale=-1.0), waits=[t1_])
        t3_ = act(lambda e: e.activation(pool1[0:32, PB:PB + TT], pool1[0:32, PC:PC + TT], AF.Ln,
                                         bias=cns[0:32, 319:320]), waits=[t2_])
        t4_ = dve(lambda e: e.tensor_relu(pool1[0:32, PA:PA + TT], pool1[0:32, PA:PA + TT]),
                  waits=tmax([ev_dt], [t1_]))
        t5_ = dve(lambda e: e.tensor_add(pool1[0:32, PC:PC + TT], pool1[0:32, PB:PB + TT],
                                         pool1[0:32, PA:PA + TT]), waits=tmax([t3_], [t4_]))
        t6_ = dve(lambda e: e.tensor_scalar_mul(pool1[0:32, PA:PA + TT], pool1[0:32, PC:PC + TT],
                                                cns[0:32, CA + 1:CA + 2]), waits=tmax([t5_], t_const))
        tscan = []
        for cc in range(NSC):
            tscan.append(dve(lambda e, cc=cc: e.tensor_tensor_scan(
                pool1[0:32, PD + cc * SC:PD + (cc + 1) * SC], onesb[0:32, 0:SC],
                pool1[0:32, PA + cc * SC:PA + (cc + 1) * SC], 0.0, OP.mult, OP.add),
                waits=tmax([t6_], t_c0)))
        t7_ = tmax(*tscan)
        t8_ = dve(lambda e: e.tensor_scalar_mul(pool1[0:32, PE_:PE_ + TT], pool1[0:32, PD:PD + TT],
                                                -1.0), waits=t7_)
        t9_ = act(lambda e: e.activation(pool1[0:32, PF:PF + TT], pool1[0:32, PE_:PE_ + TT], AF.Exp),
                  waits=[t8_])
        te2 = []
        for cc in range(NSC):
            ta_ = dve(lambda e, cc=cc: e.tensor_scalar(
                pool1[0:32, PA + cc * SC:PA + (cc + 1) * SC],
                pool1[0:32, PD + cc * SC:PD + (cc + 1) * SC],
                pool1[0:32, PD + cc * SC + SC - 1:PD + cc * SC + SC], None, OP.subtract), waits=t7_)
            te2.append(act(lambda e, cc=cc: e.activation(
                pool1[0:32, PG + cc * SC:PG + (cc + 1) * SC],
                pool1[0:32, PA + cc * SC:PA + (cc + 1) * SC], AF.Exp), waits=[ta_]))
        bufs["pool1"].wrote(tmax([t9_], *te2))

        if stop_phase <= 5:
            continue
        for cc in range(NSC):
            if stop_phase == 55 and cc > 0:
                break
            pg = st_state["ping"]
            st_prev = ST[:, pg * 2048:(pg + 1) * 2048]
            st_new = ST[:, (1 - pg) * 2048:(2 - pg) * 2048]
            st_ready = st_state["ready"]
            b_ = bank_next()
            tr(ps[b_][:, 0:32], pool1[0:32, PC + cc * SC:PC + (cc + 1) * SC], 32,
               waits=tmax([t5_], bank_tok[b_], bufs["dtT"].wr(), t_const))
            tr(ps[b_][:, 32:64], pool1[0:32, PG + cc * SC:PG + (cc + 1) * SC], 32, waits=tmax(*te2))
            mm(ps[b_][:, 128:256], xbc_t(16, 3 + cc * SC, 3 + (cc + 1) * SC),
               xbc_t(17, 3 + cc * SC, 3 + (cc + 1) * SC), True, True,
               waits=tmax([conv_toks[16]], [conv_toks[17]]))
            tt5 = tr(ps[b_][:, 256:384], xbc_t(16, 3 + cc * SC, 3 + (cc + 1) * SC), 128, inc=sPE)
            ee1 = dve(lambda e, b_=b_: e.tensor_copy(dtT[:, 0:32], ps[b_][:, 0:32]), waits=[tt5])
            ee2 = dve(lambda e, b_=b_: e.tensor_copy(dtT[:, 32:64], ps[b_][:, 32:64]))
            ee4 = dve(lambda e, b_=b_: e.tensor_copy(Gsb[:], ps[b_][:, 128:256]), waits=bufs["Gsb"].wr())
            ee5 = dve(lambda e, b_=b_: e.tensor_copy(BTsb[:], ps[b_][:, 256:384]), waits=bufs["BTsb"].wr())
            ee6 = dve(lambda e, cc=cc: e.tensor_copy(Csb[:], xbc_t(17, 3 + cc * SC, 3 + (cc + 1) * SC)),
                      waits=tmax([conv_toks[17]], bufs["Gsb"].r))
            if stop_phase == 55 and cc == 0:
                dve(lambda e: e.tensor_copy(sqsb[:, 128:256], xbc_t(17, 3, 3 + SC)),
                    waits=[conv_toks[17]])
                act(lambda e: e.activation(sqsb[:, 256:384], xbc_t(17, 3, 3 + SC), AF.Identity),
                    waits=[conv_toks[17]])
            bank_tok[b_] = [ee5]
            bufs["dtT"].wrote(tmax([ee1], [ee2]))
            bufs["Gsb"].wrote(ee4)
            bufs["BTsb"].wrote(ee5)
            xd_toks = []
            for ff in range(0, 16, 4):
                b_ = bank_next()
                tok_tr = None
                for q in range(4):
                    f = ff + q
                    tok_tr = tr(ps[b_][:, q * 128:(q + 1) * 128],
                                xbc_t(f, 3 + cc * SC, 3 + (cc + 1) * SC), 128,
                                waits=tmax([conv_toks[f]], bank_tok[b_], bufs["XdT"].wr()) if q == 0 else (),
                                inc=(sPE if q == 3 else None))
                for q in range(4):
                    f = ff + q
                    for hh in range(2):
                        xd_toks.append(dve(lambda e, b_=b_, q=q, f=f, hh=hh: e.tensor_scalar(
                            XdT[:, f * SC + hh * 64:f * SC + hh * 64 + 64],
                            ps[b_][:, q * 128 + hh * 64:q * 128 + hh * 64 + 64],
                            dtT[:, 2 * f + hh:2 * f + hh + 1], None, OP.mult),
                            waits=tmax([tok_tr], [ee1])))
                bank_tok[b_] = [xd_toks[-1]]
            t_xd = tmax(*xd_toks)
            bufs["XdT"].wrote(t_xd)
            mg_tok = [[], []]
            ce_tok = [[], []]
            mg_users = [[], []]
            ce_users = [[], []]
            y_evacs = []
            new_ready = []
            wxd_rd = [[], []]
            eb_evs = []
            fdma_toks = []
            for qtr in range(4):
                s_ = 0
                fwq = tmax(flat_users[s_], t_c0)
                tfa = dma("gpsimd", flat[0:1, 0:1024],
                          pool1[8 * qtr:8 + 8 * qtr, PD + cc * SC:PD + (cc + 1) * SC],
                          waits=tmax(t7_, fwq))
                tfb = dma("gpsimd", flat[1:2, 1024:2048],
                          pool1[8 * qtr:8 + 8 * qtr, PE_ + cc * SC:PE_ + (cc + 1) * SC],
                          waits=tmax([t8_], fwq))
                tfc = dma("gpsimd", flat[0:1, 2048:3072],
                          pool1[8 * qtr:8 + 8 * qtr, PF + cc * SC:PF + (cc + 1) * SC],
                          waits=tmax([t9_], fwq))
                t_flat = tmax([tfa], [tfb], [tfc])
                fdma_toks.append(t_flat)
                flat_users[s_] = []
                b_eb = bank_next()
                eb0 = flat[0:1, 2048 + SC - 1:2048 + SC]
                ebrhs = bass.AP(eb0.tensor, eb0.offset, [eb0.ap[0], [SC, 8]])
                teb = mm(ps[b_eb][:, 0:8], onesb[0:1, 0:128], ebrhs, True, True,
                         waits=tmax(t_flat, bank_tok[b_eb]), inc=sPE)
                flat_users[s_].append(teb)
                ebv = dve(lambda e, b_eb=b_eb, qtr=qtr: e.tensor_copy(
                    ebc[:, 8 * qtr:8 * qtr + 8], ps[b_eb][:, 0:8]),
                    waits=tmax([teb], bufs["ebc"].wr() if qtr == 0 else []))
                bank_tok[b_eb] = [ebv]
                eb_evs.append(ebv)
                for jg in range(2):
                    gidx = 2 * qtr + jg
                    sl = gidx % 2
                    b_ = bank_next()
                    last = None
                    for q in range(4):
                        h = gidx * 4 + q
                        i8 = h % 8
                        last = mm(ps[b_][:, q * 128:(q + 1) * 128],
                                  flat[0:2, i8 * SC:(i8 + 1) * SC],
                                  flat[0:2, 1024 + i8 * SC:1024 + (i8 + 1) * SC],
                                  start=(q == 0), stop=(q == 3),
                                  waits=tmax(t_flat, bank_tok[b_]) if q == 0 else (),
                                  inc=(sPE if q == 3 else None))
                    flat_users[s_].append(last)
                    tma_ = dve(lambda e, b_=b_, sl=sl: e.tensor_tensor(
                        MgA[:, sl * 512:(sl + 1) * 512], ps[b_][:, 0:512], maskb[:], OP.add),
                        waits=tmax([last], t_c0, mg_users[sl]))
                    bank_tok[b_] = [tma_]
                    tmb_ = act(lambda e, sl=sl: e.activation(
                        MgB[:, sl * 512:(sl + 1) * 512], MgA[:, sl * 512:(sl + 1) * 512], AF.Exp),
                        waits=tmax([tma_], mg_users[sl]))
                    tmc_ = dve(lambda e, sl=sl: e.tensor_tensor(
                        r4(MgB[:, sl * 512:(sl + 1) * 512]), r4(MgB[:, sl * 512:(sl + 1) * 512]),
                        bcN(Gsb[:], 4), OP.mult), waits=tmax([tmb_], [ee4]))
                    mg_tok[sl] = [tmc_]
                    mg_users[sl] = []
                    b2_ = bank_next()
                    ce_mm = mm(ps[b2_][:, 0:512], onesb[0:1, 0:128],
                               flat[0:1, 2048 + jg * 512:2048 + (jg + 1) * 512],
                               True, True,
                               waits=tmax(t_flat, bank_tok[b2_], ce_users[sl]), inc=sPE)
                    flat_users[s_].append(ce_mm)
                    ps_b2 = ps[b2_][:, 0:512]
                    tcd = dve(lambda e, ps_b2=ps_b2, sl=sl: e.tensor_tensor(
                        r4(CEg[:, sl * 512:(sl + 1) * 512]),
                        bass.AP(ps_b2.tensor, ps_b2.offset, [ps_b2.ap[0], [128, 4], [1, 128]]),
                        bcN(Csb[:], 4), OP.mult),
                        waits=tmax([ce_mm], [ee6], ce_users[sl]))
                    if stop_phase == 55 and gidx == 7:
                        for j4 in range(4):
                            dve(lambda e, sl=sl, j4=j4, cc=cc: e.tensor_tensor(
                                szsb[:, j4 * 128:(j4 + 1) * 128],
                                MgA[:, sl * 512 + j4 * 128:sl * 512 + (j4 + 1) * 128],
                                xbc_t(17, 3 + cc * SC, 3 + (cc + 1) * SC), OP.mult))
                        dve(lambda e, sl=sl: e.tensor_copy(szsb[:, 512:1024],
                                                           MgA[:, sl * 512:(sl + 1) * 512]))
                        dve(lambda e, cc=cc: e.tensor_copy(sqsb[:, 0:128],
                                                    xbc_t(17, 3 + cc * SC, 3 + (cc + 1) * SC)))
                        dve(lambda e: e.tensor_copy(sqsb[:, 384:512], Csb[:]))
                    bank_tok[b2_] = [tcd]
                    ce_tok[sl] = [tcd]
                    ce_users[sl] = []
                yb = qtr
                b3_ = bank_next()
                nmm = 0
                last = None
                for pq in range(4):
                    f = yb * 4 + pq
                    for hh in range(2):
                        h = 2 * f + hh
                        sl_h = (h // 4) % 2
                        w0 = tmax(mg_tok[sl_h], ce_tok[sl_h])
                        if nmm == 0:
                            w0 = tmax(w0, t_xd, bank_tok[b3_], st_ready, bufs["Ybuf"].wr())
                        mm(ps[b3_][hh * 64:hh * 64 + 64, pq * 128:(pq + 1) * 128],
                           XdT[:, f * SC + hh * 64:f * SC + hh * 64 + 64],
                           MgB[:, sl_h * 512 + (h % 4) * 128:sl_h * 512 + (h % 4) * 128 + 128],
                           start=(pq == 0), stop=False, waits=w0)
                        nmm += 1
                        last = mm(ps[b3_][hh * 64:hh * 64 + 64, pq * 128:(pq + 1) * 128],
                                  st_prev[:, f * 128 + hh * 64:f * 128 + hh * 64 + 64],
                                  CEg[:, sl_h * 512 + (h % 4) * 128:sl_h * 512 + (h % 4) * 128 + 128],
                                  start=False, stop=(pq == 3),
                                  inc=(sPE if (pq == 3 and hh == 1) else None))
                        nmm += 1
                bank_last = last
                for sl2 in (0, 1):
                    mg_users[sl2] = [bank_last]
                    ce_users[sl2] = [bank_last]
                yevs = []
                for pq in range(4):
                    f = yb * 4 + pq
                    yevs.append(dve(lambda e, b3_=b3_, pq=pq, f=f, cc=cc: e.scalar_tensor_tensor(
                        Ybuf[:, f * TT + cc * SC:f * TT + (cc + 1) * SC],
                        xbc_t(f, 3 + cc * SC, 3 + (cc + 1) * SC),
                        cns[:, CDSK + f:CDSK + f + 1],
                        ps[b3_][:, pq * 128:(pq + 1) * 128], OP.mult, OP.add),
                        waits=tmax([last], t_const, bufs["Ybuf"].wr())))
                y_evacs += yevs
                lastu = None
                for pq in range(4):
                    f = yb * 4 + pq
                    wsl = pq % 2
                    wts = []
                    for hh in range(2):
                        wts.append(dve(lambda e, f=f, hh=hh, wsl=wsl: e.tensor_scalar(
                            wXd[:, wsl * SC + hh * 64:wsl * SC + hh * 64 + 64],
                            XdT[:, f * SC + hh * 64:f * SC + hh * 64 + 64],
                            dtT[:, 32 + 2 * f + hh:32 + 2 * f + hh + 1], None, OP.mult),
                            waits=tmax(t_xd, [ee2], wxd_rd[wsl], bufs["wXd"].wr())))
                    lastu = mm(ps[b3_][:, pq * 128:(pq + 1) * 128], BTsb[:],
                               wXd[:, wsl * SC:(wsl + 1) * SC],
                               start=(pq == 0), stop=(pq == 3),
                               waits=tmax([ee5], tmax(*wts), tmax(*yevs)) if pq == 0
                               else tmax(tmax(*wts)),
                               inc=sPE)
                    wxd_rd[wsl] = [lastu]
                stts = []
                for pq in range(4):
                    f = yb * 4 + pq
                    for hh in range(2):
                        h = 2 * f + hh
                        stts.append(dve(lambda e, b3_=b3_, f=f, hh=hh, pq=pq, h=h,
                                        st_new=st_new, st_prev=st_prev:
                                        e.scalar_tensor_tensor(
                            st_new[:, f * 128 + hh * 64:f * 128 + hh * 64 + 64],
                            st_prev[:, f * 128 + hh * 64:f * 128 + hh * 64 + 64],
                            ebc[:, h:h + 1],
                            ps[b3_][:, pq * 128 + hh * 64:pq * 128 + hh * 64 + 64],
                            OP.mult, OP.add), waits=tmax([lastu], [eb_evs[qtr]])))
                bank_tok[b3_] = [stts[-1]]
                new_ready += stts
            st_state["ready"] = tmax(*new_ready)
            st_state["ping"] = 1 - st_state["ping"]
            bufs["Ybuf"].wrote(tmax(*y_evacs))
            bufs["wXd"].wrote(tmax(*new_ready))
            bufs["ebc"].wrote(tmax(*eb_evs))
            bufs["pool1"].read(tmax([ee1], *fdma_toks))
        t_scan = tmax(bufs["Ybuf"].w)

        if stop_phase <= 6 or stop_phase == 55:
            continue
        b_rms = bank_next()
        held.add(b_rms)
        rms_first = [True]
        gate_toks = []
        rm_last = [None]
        for f in range(16):
            def z_evac(b_, last, f=f):
                return act(lambda e: e.activation(szsb[:, (f % 2) * 512:(f % 2) * 512 + TT],
                                                  ps[b_][:, 0:TT], AF.Silu),
                           waits=tmax([last], bufs["szsb"].wr()))
            evz = inproj_m(f, z_evac)
            bufs["szsb"].wrote(evz)
            tg = dve(lambda e, f=f: e.tensor_tensor(
                Ybuf[:, f * TT:(f + 1) * TT], Ybuf[:, f * TT:(f + 1) * TT],
                szsb[:, (f % 2) * 512:(f % 2) * 512 + TT], OP.mult),
                waits=tmax([evz], t_scan))
            bufs["szsb"].read(tg)
            tq = act(lambda e, f=f: e.activation(sqsb[:, (f % 2) * 512:(f % 2) * 512 + TT],
                                                 Ybuf[:, f * TT:(f + 1) * TT], AF.Square),
                     waits=tmax([tg], bufs["sqsb"].wr()))
            bufs["sqsb"].wrote(tq)
            rm_last[0] = mm(ps[b_rms][0:1, 0:TT], onesb[:, 0:1],
                            sqsb[:, (f % 2) * 512:(f % 2) * 512 + TT],
                            start=(f == 0), stop=(f == 15),
                            waits=tmax([tq], bank_tok[b_rms]) if f == 0 else [tq],
                            inc=(sPE if f == 15 else None))
            bufs["sqsb"].read(rm_last[0])
            ts_ = dve(lambda e, f=f: e.tensor_scalar_mul(
                Ybuf[:, f * TT:(f + 1) * TT], Ybuf[:, f * TT:(f + 1) * TT],
                cns[:, CRW + f:CRW + f + 1]), waits=tmax([tq], t_const))
            gate_toks.append(ts_)
        t_gate = tmax(*gate_toks)
        bufs["Ybuf"].wrote(t_gate)
        pr = act(lambda e, b_rms=b_rms: e.activation(pool1[0:1, PB:PB + TT], ps[b_rms][0:1, 0:TT], AF.Sqrt,
                                        scale=1.0 / DI, bias=cns[0:1, 318:319]),
                 waits=tmax([rm_last[0]], bufs["pool1"].wr()))
        pr2 = dve(lambda e: e.reciprocal(pool1[0:1, PA:PA + 512], pool1[0:1, PB:PB + TT]), waits=[pr])
        bank_tok[b_rms] = [pr]
        held.discard(b_rms)
        bufs["pool1"].wrote(tmax([pr2]))
        b_ = bank_next()
        tok_tr = None
        for g in range(4):
            tok_tr = tr(ps[b_][:, g * 128:g * 128 + 1], pool1[0:1, PA + g * 128:PA + (g + 1) * 128], 1,
                        waits=tmax([pr2], bank_tok[b_]) if g == 0 else (),
                        inc=(sPE if g == 3 else None))
        tev = []
        for g in range(4):
            tev.append(dve(lambda e, b_=b_, g=g: e.tensor_copy(colp[:, 20 + g:21 + g],
                                                               ps[b_][:, g * 128:g * 128 + 1]),
                           waits=tmax([tok_tr], bufs["colp"].wr())))
        bank_tok[b_] = [tev[-1]]
        bufs["colp"].wrote(tmax(*tev))
        t_rstd_tm = tmax(*tev)

        if stop_phase <= 7:
            continue
        attn_ev = []
        op_mm_last = []
        for m in range(8):
            si1, slot1 = slab_next()
            tw1 = dma("sync", slot1[:, 0:1024], dram["w_outproj"][:, m * 2048:m * 2048 + 1024],
                      waits=slab_tok[si1])
            si2, slot2 = slab_next()
            tw2 = dma("sync", slot2[:, 0:1024], dram["w_outproj"][:, m * 2048 + 1024:(m + 1) * 2048],
                      waits=slab_tok[si2])
            b_ = bank_next()
            last = None
            for k in range(16):
                sl_ap = (slot1 if k < 8 else slot2)[:, (k % 8) * 128:(k % 8) * 128 + 128]
                last = mm(ps[b_][:, 0:TT], sl_ap, Ybuf[:, k * TT:(k + 1) * TT],
                          start=(k == 0), stop=(k == 15),
                          waits=tmax([tw1], [tw2], bank_tok[b_], t_gate) if k == 0 else (),
                          inc=(sPE if k == 15 else None))
            ev = dve(lambda e, b_=b_, m=m: e.tensor_copy(fmS[:, m * TT:(m + 1) * TT], ps[b_][:, 0:TT]),
                     waits=tmax([last], bufs["fmS"].wr()))
            bank_tok[b_] = [ev]
            slab_tok[si1] = [last]
            slab_tok[si2] = [last]
            attn_ev.append(ev)
            op_mm_last.append(last)
        bufs["fmS"].wrote(tmax(*attn_ev))
        bufs["Ybuf"].read(tmax(*op_mm_last))
        t_attn = tmax(*attn_ev)

        out_toks = []
        for g in range(4):
            for jj in range(0, 8, 4):
                b_ = bank_next()
                tok_tr = None
                for q in range(4):
                    j = jj + q
                    tok_tr = tr(ps[b_][:, q * 128:(q + 1) * 128],
                                fmS[:, j * TT + g * 128:j * TT + (g + 1) * 128], 128,
                                waits=tmax(t_attn, bank_tok[b_]) if q == 0 else (),
                                inc=(sPE if q == 3 else None))
                for q in range(4):
                    j = jj + q
                    out_toks.append(dve(lambda e, b_=b_, q=q, j=j, g=g: e.scalar_tensor_tensor(
                        tmB[:, g * D + j * 128:g * D + (j + 1) * 128],
                        ps[b_][:, q * 128:(q + 1) * 128], colp[:, 20 + g:21 + g],
                        tmA[:, g * D + j * 128:g * D + (j + 1) * 128], OP.mult, OP.add),
                        waits=tmax([tok_tr], t_rstd_tm, bufs["tmB"].wr())))
                bank_tok[b_] = [out_toks[-1]]
        t_out = tmax(*out_toks)
        bufs["tmB"].wrote(t_out)
        bufs["tmA"].read(t_out)
        bufs["fmS"].read(t_out)

        if stop_phase <= 8:
            continue
        tf, _ = ln_stats(tmB, bufs["tmB"].rd())
        h1tm = []
        for g in range(4):
            h1tm.append(dve(lambda e, g=g: e.tensor_scalar(
                Ybuf[:, g * D:(g + 1) * D], tmB[:, g * D:(g + 1) * D],
                colp[:, 8 + g:9 + g], colp[:, 28 + g:29 + g], OP.subtract, OP.mult),
                waits=tmax([tf], bufs["tmB"].rd(), bufs["Ybuf"].wr())))
        bufs["colp"].wrote(tmax([tf], *h1tm))
        bufs["Ybuf"].wrote(tmax(*h1tm))
        def ev_h1(b_, q, j, g, tok_tr):
            return dve(lambda e: e.tensor_scalar(
                Ybuf[:, 4096 + j * TT + g * 128:4096 + j * TT + (g + 1) * 128],
                ps[b_][:, q * 128:(q + 1) * 128], gam(j), bet(j), OP.mult, OP.add),
                waits=[tok_tr])
        h1fm = transpose_1024(Ybuf, ev_h1, waits_in=tmax(*h1tm))
        t_h1 = tmax(*h1fm)
        bufs["Ybuf"].wrote(t_h1)
        xo_toks = []
        for g in range(4):
            xo_toks.append(dve(lambda e, g=g: e.tensor_add(
                fmS[:, g * D:(g + 1) * D], tmA[:, g * D:(g + 1) * D], tmB[:, g * D:(g + 1) * D]),
                waits=tmax(bufs["tmA"].rd(), bufs["tmB"].rd(), bufs["fmS"].wr())))
        t_xo = tmax(*xo_toks)
        bufs["fmS"].wrote(t_xo)
        bufs["tmA"].read(t_xo)
        bufs["tmB"].read(t_xo)

        def h2slot(m1, half):
            c = m1 // 2
            off = 3 + (m1 % 2) * 256 + half * 0
            return xbc[:, c * XW + off:c * XW + off + 256]

        h3_done = []
        for half in range(2):
            h2_toks = {}
            for m1 in range(32):
                si, slot = slab_next()
                tw = dma("sync", slot[:, 0:1024], dram["w_mlp1"][:, m1 * 1024:(m1 + 1) * 1024],
                         waits=slab_tok[si])
                b_ = bank_next()
                last = None
                for k in range(NKD):
                    last = mm(ps[b_][:, 0:256], slot[:, k * 128:(k + 1) * 128],
                              Ybuf[:, 4096 + k * TT + half * 256:4096 + k * TT + half * 256 + 256],
                              start=(k == 0), stop=(k == NKD - 1),
                              waits=tmax([tw], bank_tok[b_], t_h1,
                                         bufs["xbc"].wr() if (half == 0 and m1 == 0) else []) if k == 0 else (),
                              inc=(sPE if k == NKD - 1 else None))
                slab_tok[si] = [last]
                hev = act(lambda e, b_=b_, m1=m1, half=half: e.activation(
                    h2slot(m1, half), ps[b_][:, 0:256], AF.Relu,
                    bias=cns[:, CB1 + m1:CB1 + m1 + 1]),
                    waits=tmax([last], t_const))
                bank_tok[b_] = [hev]
                h2_toks[m1] = hev
            for mo in range(8):
                b_ = bank_next()
                last = None
                for kk in range(0, 32, 8):
                    si, slot = slab_next()
                    tw = dma("sync", slot[:, 0:1024],
                             dram["w_mlp2"][:, mo * 4096 + kk * 128:mo * 4096 + (kk + 8) * 128],
                             waits=slab_tok[si])
                    for k2 in range(8):
                        k = kk + k2
                        last = mm(ps[b_][:, 0:256], slot[:, k2 * 128:(k2 + 1) * 128],
                                  h2slot(k, half), start=(k == 0), stop=(k == 31),
                                  waits=tmax([tw], [h2_toks[k]],
                                             bank_tok[b_] if k == 0 else []),
                                  inc=(sPE if k2 == 7 else None))
                    slab_tok[si] = [last]
                he = dve(lambda e, b_=b_, mo=mo, half=half: e.tensor_scalar(
                    Ybuf[:, mo * TT + half * 256:mo * TT + half * 256 + 256], ps[b_][:, 0:256],
                    cns[:, CBB + mo:CBB + mo + 1], None, OP.add),
                    waits=tmax([last], t_const))
                bank_tok[b_] = [he]
                h3_done.append(he)
        t_h3 = tmax(*h3_done)
        bufs["Ybuf"].wrote(t_h3)
        bufs["xbc"].wrote(t_h3)

        def ev_s(b_, q, j, g, tok_tr):
            return dve(lambda e: e.tensor_add(
                tmA[:, g * D + j * 128:g * D + (j + 1) * 128], ps[b_][:, q * 128:(q + 1) * 128],
                fmS[:, g * D + j * 128:g * D + (j + 1) * 128]),
                waits=tmax([tok_tr], t_xo, bufs["tmA"].wr()))
        s_toks = transpose_1024(Ybuf, ev_s, waits_in=tmax(t_h3, bufs["Ybuf"].rd()), fm_src=True)
        t_s = tmax(*s_toks)
        bufs["tmA"].wrote(t_s)
        bufs["fmS"].read(t_s)
        bufs["Ybuf"].read(t_s)

        if stop_phase <= 9:
            continue
        tf, _ = ln_stats(tmA, bufs["tmA"].rd())
        x2tm = ln_apply(tmB, tmA, tmax([tf], bufs["tmA"].rd(), bufs["tmB"].wr()))
        bufs["colp"].wrote(tmax([tf], *x2tm))
        bufs["tmB"].wrote(tmax(*x2tm))
        bufs["tmA"].read(tmax(*x2tm))
        def ev_x2(b_, q, j, g, tok_tr):
            return dve(lambda e: e.tensor_copy(
                fmS[:, j * TT + g * 128:j * TT + (g + 1) * 128], ps[b_][:, q * 128:(q + 1) * 128]),
                waits=tmax([tok_tr], bufs["fmS"].wr()))
        x2fm = transpose_1024(tmB, ev_x2, waits_in=tmax(tmax(*x2tm), bufs["tmB"].rd()))
        t_x2fm = tmax(*x2fm)
        bufs["fmS"].wrote(t_x2fm)
        bufs["tmB"].read(t_x2fm)

        if stop_phase <= 10:
            continue
        y2_ev = []
        ffn_mm_last = []
        for mo in range(8):
            si, slot = slab_next()
            tw = dma("sync", slot[:, 0:1024], weff_s[:, mo * 1024:(mo + 1) * 1024],
                     waits=tmax(slab_tok[si], t_weff_done))
            b_ = bank_next()
            last = None
            for k in range(NKD):
                last = mm(ps[b_][:, 0:TT], slot[:, k * 128:(k + 1) * 128], fmS[:, k * TT:(k + 1) * TT],
                          start=(k == 0), stop=(k == NKD - 1),
                          waits=tmax([tw], bank_tok[b_], t_x2fm) if k == 0 else (),
                          inc=(sPE if k == NKD - 1 else None))
            slab_tok[si] = [last]
            ev = dve(lambda e, b_=b_, mo=mo: e.tensor_scalar(
                Ybuf[:, mo * TT:(mo + 1) * TT], ps[b_][:, 0:TT],
                cns[:, CBB + 8 + mo:CBB + 8 + mo + 1], None, OP.add),
                waits=tmax([last], t_const, bufs["Ybuf"].wr()))
            bank_tok[b_] = [ev]
            y2_ev.append(ev)
            ffn_mm_last.append(last)
        t_y2 = tmax(*y2_ev)
        bufs["Ybuf"].wrote(t_y2)
        bufs["fmS"].read(tmax(*ffn_mm_last))
        def ev_s2(b_, q, j, g, tok_tr):
            return dve(lambda e: e.tensor_add(
                tmA[:, g * D + j * 128:g * D + (j + 1) * 128], ps[b_][:, q * 128:(q + 1) * 128],
                tmB[:, g * D + j * 128:g * D + (j + 1) * 128]),
                waits=tmax([tok_tr], bufs["tmB"].rd(), bufs["tmA"].wr()))
        s2_toks = transpose_1024(Ybuf, ev_s2, waits_in=tmax(t_y2), fm_src=True)
        t_s2 = tmax(*s2_toks)
        bufs["tmA"].wrote(t_s2)
        bufs["tmB"].read(t_s2)
        bufs["Ybuf"].read(t_s2)

        tf, _ = ln_stats(tmA, bufs["tmA"].rd())
        x3 = ln_apply(tmB, tmA, tmax([tf], bufs["tmA"].rd(), bufs["tmB"].wr()))
        bufs["colp"].wrote(tmax([tf], *x3))
        bufs["tmB"].wrote(tmax(*x3))
        bufs["tmA"].read(tmax(*x3))
        touts = []
        for g in range(4):
            touts.append(dma("gpsimd", out_d[seq, t0 + g * 128:t0 + (g + 1) * 128, :],
                             tmB[:, g * D:(g + 1) * D], waits=[x3[g]]))
        bufs["tmB"].read(tmax(*touts))
        t_out_final[0] = tmax(*touts)

    if taps:
        for tname, shape, getter in taps:
            if getter is None:
                tt_ = dma("gpsimd", tap_outs[tname], weff_s[:],
                          waits=tmax(t_out_final[0], t_weff_done))
                t_out_final[0] = tmax(t_out_final[0], [tt_])
                continue
            src = getter(dict(tmA=tmA, tmB=tmB, fmS=fmS, Ybuf=Ybuf, xbc=xbc, pool1=pool1,
                              dtT=dtT, flat=flat, XdT=XdT, MgA=MgA, MgB=MgB, CEg=CEg,
                              Gsb=Gsb, BTsb=BTsb, wXd=wXd, ST=ST, ebc=ebc, szsb=szsb,
                              sqsb=sqsb, colp=colp, cns=cns))
            tt_ = dma("gpsimd", tap_outs[tname], src,
                      waits=tmax(t_out_final[0], [(sDVE, sDVE.count)], [(sACT, sACT.count)],
                                 [(sPE, sPE.count)]))
            t_out_final[0] = tmax(t_out_final[0], [tt_])

    fin = tmax(t_out_final[0], [(sDVE, sDVE.count)], [(sACT, sACT.count)],
               [(sPE, sPE.count)], [(sIO, sIO.count)], [(sWL, sWL.count)])
    P.emit("gpsimd", lambda e: e.nop(), waits=fin)

    with es, nc.Block() as block:
        for eng_name in ENGINES:
            def make_body(ops):
                def body(eng):
                    for fn, waits, inc, incn in ops:
                        for s, v in waits:
                            eng.wait_ge(s.h, v)
                        instr = fn(eng)
                        if inc is not None and instr is not None:
                            instr.then_inc(inc.h, incn)
                return body
            getattr(block, eng_name)(make_body(P.ops[eng_name]))
    return nc


def kernel(**inputs):
    x = np.asarray(inputs["L0_attn_x"], np.float32)
    packed = host_pack(inputs)
    nc = build_program()
    in_maps = []
    for c in range(8):
        m = dict(packed)
        m["x"] = np.ascontiguousarray(x[2 * c:2 * c + 2])
        in_maps.append(m)
    res = run_bass_kernel_spmd(nc, in_maps, core_ids=list(range(8)))
    out = np.concatenate([np.asarray(r["out"]).reshape(1, SEQ, L, D) for r in res.results], axis=0)
    return np.ascontiguousarray(out.reshape(16, L, D)).astype(np.float32)
